# version 30
# baseline (speedup 1.0000x reference)
import numpy as np
from contextlib import ExitStack

import concourse.bass as bass
import concourse.mybir as mybir
from concourse.bass_utils import run_bass_kernel_spmd

F32 = mybir.dt.float32
BF16 = mybir.dt.bfloat16
AF = mybir.ActivationFunctionType
ALU = mybir.AluOpType

NCORES = 8
D = 1024
DI = 512
DH = 4096
T = 2048
TH = T // 2
NT128 = TH // 128
NTT = TH // 512
NDC = D // 128
NIC = DI // 128
NHC = DH // 128
CHUNK = 128
NCH = TH // CHUNK
LN_EPS = 1e-5
CLAMP = 87.0

ENGINES = ("sync", "gpsimd", "tensor", "vector", "scalar")


SELF_SEM = {"tensor": "pe", "vector": "ve", "scalar": "ac", "gpsimd": "gp"}


class B:

    def __init__(self, nc, ctx):
        self.nc = nc
        self.ctx = ctx
        self.ops = {e: [] for e in ENGINES}
        self.sems = {}
        self.cnt = {}
        self.lastwait = {e: {} for e in ENGINES}
        self.bar_toks = []

    def sem(self, name):
        if name not in self.sems:
            self.sems[name] = self.ctx.enter_context(self.nc.semaphore(name))
            self.cnt[name] = 0
        return self.sems[name]

    def wait(self, eng, *toks):
        flat = []
        for t in toks:
            if isinstance(t, list):
                flat.extend(t)
            else:
                flat.append(t)
        for t in flat:
            if t is None:
                continue
            (s, v) = t
            if v <= 0 or self.lastwait[eng].get(s, -1) >= v:
                continue
            self.lastwait[eng][s] = v
            sem = self.sem(s)
            self.ops[eng].append(lambda e, sem=sem, v=v: e.wait_ge(sem, v))

    def emit(self, eng, fn, inc=None, by=1):
        if inc is None and eng in SELF_SEM:
            inc = SELF_SEM[eng]
        if eng in SELF_SEM and eng != "tensor":
            self.wait(eng, self.tok(SELF_SEM[eng]))
        if inc is None:
            self.ops[eng].append(fn)
            return None
        sem = self.sem(inc)
        self.cnt[inc] += by
        tok = (inc, self.cnt[inc])
        def wrapped(e, fn=fn, sem=sem, by=by):
            fn(e).then_inc(sem, by)
        self.ops[eng].append(wrapped)
        return tok

    def tok(self, s):
        if s in SELF_SEM:
            s = SELF_SEM[s]
        return (s, self.cnt.get(s, 0))

    def barrier(self, engs=("tensor", "vector", "scalar", "gpsimd")):
        toks = [self.tok(SELF_SEM[e]) for e in engs]
        for e in engs:
            self.wait(e, *toks)
        self.bar_toks = toks
        return toks

    def dma_gate(self, eng):
        self.wait(eng, *self.bar_toks)

    def sbuf(self, name, shape, dt, ctx=None):
        return (ctx or self.ctx).enter_context(self.nc.sbuf_tensor(name, shape, dt))

    def psum(self, name, shape, dt, ctx=None):
        return (ctx or self.ctx).enter_context(self.nc.psum_tensor(name, shape, dt))

    def finalize(self, block):
        for e in ENGINES:
            ops = self.ops[e]
            getattr(block, e)(lambda eng, ops=ops: [f(eng) for f in ops] and None)


class DmaQ:

    def __init__(self, b, eng, prefix, k):
        self.b = b
        self.eng = eng
        self.names = [f"{prefix}{i}" for i in range(k)]
        for s in self.names:
            b.sem(s)

    def issue(self, fn):
        s = min(self.names, key=lambda n: self.b.cnt[n])
        self.b.wait(self.eng, self.b.tok(s))
        return self.b.emit(self.eng, fn, inc=s, by=16)

    def all_done_toks(self):
        return [self.b.tok(s) for s in self.names]


def build_program(debug_taps=()):
    nc = bass.Bass("TRN2", target_bir_lowering=False)

    x_d = nc.dram_tensor("x", [T, D], F32, kind="ExternalInput").ap()
    xf_d = nc.dram_tensor("xf", [D, T], F32, kind="ExternalInput").ap()
    wq_d = nc.dram_tensor("wq", [NIC, 128, NDC, 128], F32, kind="ExternalInput").ap()
    wk_d = nc.dram_tensor("wk", [NIC, 128, NDC, 128], F32, kind="ExternalInput").ap()
    wv_d = nc.dram_tensor("wv", [NIC, 128, NDC, 128], F32, kind="ExternalInput").ap()
    wa_d = nc.dram_tensor("wa", [NIC, 128, NDC, 128], F32, kind="ExternalInput").ap()
    wo_d = nc.dram_tensor("wo", [128, NIC, D], F32, kind="ExternalInput").ap()
    ba_d = nc.dram_tensor("ba", [DI], F32, kind="ExternalInput").ap()
    gamma_d = nc.dram_tensor("gamma", [D], F32, kind="ExternalInput").ap()
    beta_d = nc.dram_tensor("beta", [D], F32, kind="ExternalInput").ap()
    w1_d = nc.dram_tensor("w1t", [NHC, 128, NDC, 128], F32, kind="ExternalInput").ap()
    w2_d = nc.dram_tensor("w2t", [NHC, 128, NDC, 128], F32, kind="ExternalInput").ap()
    h1_d = nc.dram_tensor("h1t", [NHC, 128, NHC, 128], F32, kind="ExternalInput").ap()
    h2_d = nc.dram_tensor("h2t", [NHC, 128, NHC, 128], F32, kind="ExternalInput").ap()
    w3_d = nc.dram_tensor("w3t", [NDC, 128, NHC, 128], F32, kind="ExternalInput").ap()
    s1_d = nc.dram_tensor("s1", [DH], F32, kind="ExternalInput").ap()
    s2_d = nc.dram_tensor("s2", [DH], F32, kind="ExternalInput").ap()
    hs1_d = nc.dram_tensor("hs1", [DH], F32, kind="ExternalInput").ap()
    hs2_d = nc.dram_tensor("hs2", [DH], F32, kind="ExternalInput").ap()
    s3_d = nc.dram_tensor("s3", [D], F32, kind="ExternalInput").ap()
    out_d = nc.dram_tensor("out", [D, T], F32, kind="ExternalOutput").ap()

    taps = {}

    with ExitStack() as ctx:
        b = B(nc, ctx)
        for s in ("pe", "ve", "ac", "gp"):
            b.sem(s)
        b.SQ = DmaQ(b, "sync", "sq", 4)
        b.GQ = DmaQ(b, "gpsimd", "gq", 6)
        b.ST = DmaQ(b, "sync", "st", 2)

        def tap(name, ap, *wait_toks):
            if name not in debug_taps:
                return
            td = nc.dram_tensor("tap_" + name, list(ap.shape), ap.dtype,
                                kind="ExternalOutput").ap()
            taps[name] = td
            b.wait("sync", *wait_toks)
            tk = b.ST.issue(lambda e, td=td, ap=ap: e.dma_start(out=td, in_=ap))
            for eng in ("tensor", "vector", "scalar", "gpsimd"):
                b.wait(eng, tk)

        ident_f = b.sbuf("ident_f", [128, 128], F32)
        ident_b = b.sbuf("ident_b", [128, 128], BF16)
        mask2 = b.sbuf("mask2", [128, 128], F32)
        e2 = b.sbuf("e2", [128, 2], BF16)
        e2b = b.sbuf("e2b", [2, 128], BF16)
        zeros64 = b.sbuf("zeros64", [128, CHUNK], F32)
        ban = b.sbuf("ban", [128, NIC], F32)
        gam = b.sbuf("gam", [128, NDC], F32)
        bet = b.sbuf("bet", [128, NDC], F32)
        s1c = b.sbuf("s1c", [128, NHC], F32)
        s2c = b.sbuf("s2c", [128, NHC], F32)
        hs1c = b.sbuf("hs1c", [128, NHC], F32)
        hs2c = b.sbuf("hs2c", [128, NHC], F32)
        s3c = b.sbuf("s3c", [128, NDC], F32)
        S_sb = [b.sbuf(f"S{p}", [128, 128], F32) for p in range(NIC)]
        S_bf = [b.sbuf(f"Sbf{p}", [128, 128], BF16) for p in range(NIC)]
        stats = [b.sbuf(f"stats{i}", [128, 16], F32) for i in range(3)]
        mvcol = [b.sbuf(f"mvcol{i}", [128, 8], F32) for i in range(3)]
        cc = b.sbuf("cc", [128, 2], F32)

        vec_toks = []
        for dst, src in ((ban, ba_d), (gam, gamma_d), (bet, beta_d),
                         (s1c, s1_d), (s2c, s2_d), (hs1c, hs1_d),
                         (hs2c, hs2_d), (s3c, s3_d)):
            vec_toks.append(b.SQ.issue(lambda e, dst=dst, src=src: e.dma_start(
                out=dst[:, :], in_=src.rearrange("(c p) -> p c", p=128),
                allow_slow_non_contiguous=True)))

        def g_memset(ap, v):
            b.wait("gpsimd", b.tok("gp"))
            b.emit("gpsimd", lambda e, ap=ap, v=v: e.memset(ap, v), inc="gp")

        def g_chain(fn):
            b.wait("gpsimd", b.tok("gp"))
            return b.emit("gpsimd", fn, inc="gp")
        g_memset(zeros64[:, :], 0.0)
        g_memset(cc[:, 0:1], LN_EPS)
        g_memset(cc[:, 1:2], 1e-24)
        g_memset(ident_f[:, :], 0.0)
        g_chain(lambda e: e.affine_select(
            out=ident_f[:, :], in_=ident_f[:, :], compare_op=ALU.not_equal,
            fill=1.0, base=0, pattern=[[-1, 128]], channel_multiplier=1))
        g_memset(ident_b[:, :], 0.0)
        g_chain(lambda e: e.affine_select(
            out=ident_b[:, :], in_=ident_b[:, :], compare_op=ALU.not_equal,
            fill=1.0, base=0, pattern=[[-1, 128]], channel_multiplier=1))
        g_memset(mask2[:, :], 1.0)
        g_chain(lambda e: e.affine_select(
            out=mask2[:, :], in_=mask2[:, :], compare_op=ALU.is_ge,
            fill=0.0, base=0, pattern=[[1, 128]], channel_multiplier=-1))
        g_memset(e2[:, :], 0.0)
        g_memset(e2[0:64, 0:1], 1.0)
        g_memset(e2[64:128, 1:2], 1.0)
        g_memset(e2b[:, :], 1.0)
        g_chain(lambda e: e.affine_select(
            out=e2b[:, :], in_=e2b[:, :], compare_op=ALU.is_ge,
            fill=0.0, base=0, pattern=[[1, 128]], channel_multiplier=-64))
        g_chain(lambda e: e.affine_select(
            out=e2b[:, :], in_=e2b[:, :], compare_op=ALU.is_ge,
            fill=0.0, base=63, pattern=[[-1, 128]], channel_multiplier=64))
        for p in range(NIC):
            g_memset(S_sb[p][:, :], 0.0)
            g_memset(S_bf[p][:, :], 0.0)
        gp_tok = b.tok("gp")
        b.wait("vector", *vec_toks, gp_tok)
        b.emit("vector", lambda e: e.tensor_scalar(
            ban[:, :], ban[:, :], -1.0, None, ALU.mult), inc="ve")
        b.barrier()


        prev_store_toks = [None] * NDC
        for h in range(2):
            t0h = h * TH
            with ExitStack() as hctx:
                acc = b.sbuf(f"acc_h{h}", [128, NDC * TH], F32, hctx)
                h_sb = b.sbuf(f"hbuf_h{h}", [128, NHC * TH], BF16, hctx)
                def issue_acc_ld(acc=acc, t0h=t0h, pst=prev_store_toks):
                    toks = []
                    for dc in range(NDC):
                        dst = acc[:, dc * TH:(dc + 1) * TH]
                        fsrc = xf_d[dc * 128:(dc + 1) * 128, t0h:t0h + TH]
                        b.wait("sync", pst[dc])
                        toks.append(b.SQ.issue(lambda e, dst=dst, fsrc=fsrc:
                                               e.dma_start(out=dst, in_=fsrc)))
                    return toks

                with ExitStack() as xctx:
                    xn2 = b.sbuf(f"xn2_h{h}", [128, NDC * TH], BF16, xctx)

                    with ExitStack() as lctx:
                        xn = b.sbuf(f"xn_h{h}", [128, NDC * TH], BF16, lctx)
                        wub = [b.sbuf(f"wub{i}_h{h}", [128, NDC * 128], BF16,
                                      lctx) for i in range(6)]
                        _ln_phase(b, nc, h, t0h, x_d, stats, mvcol, cc,
                                  ident_f, ident_b, gam, bet,
                                  xn, xn2, acc, lctx, issue_acc_ld,
                                  getattr(b, "h2_done", None) if h else None)
                        b.barrier()
                        tap(f"xn_h{h}", xn[:, :], b.tok("ac"), b.tok("ve"))
                        tap(f"xn2_h{h}", xn2[:, :], b.tok("ve"))
                        tap(f"acc_ln_h{h}", acc[:, :], b.tok("ve"))

                        _u_stage(b, nc, h, w1_d, w2_d, s1c, s2c, xn, h_sb, wub,
                                 getattr(b, "h2_done", None) if h else None)
                        b.barrier()
                        tap(f"hbuf_h{h}", h_sb[:, :], b.tok("ve"))

                    _attn_phase(b, nc, h, wq_d, wk_d, wv_d, wa_d, wo_d, ban, cc,
                                e2, e2b, zeros64, mask2, ident_b, xn2, acc,
                                S_sb, S_bf, tap)
                    b.barrier()
                    tap(f"acc_attn_h{h}", acc[:, :], b.tok("ve"))

                prev_store_toks = _ffn_tail(b, nc, h, h1_d, h2_d, w3_d,
                                            hs1c, hs2c, s3c, h_sb, acc, tap,
                                            t0h, out_d)
                tap(f"acc_out_h{h}", acc[:, :], b.tok("ve"))

        b.wait("sync", *b.ST.all_done_toks())

        block = ctx.enter_context(nc.Block())
        b.finalize(block)

    return nc, taps


def _ln_phase(b, nc, h, t0h, x_d, stats, mvcol, cc,
              ident_f, ident_b, gam, bet, xn, xn2, acc, lctx, issue_acc_ld,
              xguard):
    acc_ld = []
    if xguard:
        b.wait("vector", xguard)
        b.wait("scalar", xguard)
    with ExitStack() as pctx:
        x_tm = [b.sbuf(f"xin{i}_h{h}", [128, D], F32, pctx) for i in range(4)]
        x_busy = [None] * 4
        xn_tile = [b.sbuf(f"xn_tile{i}_h{h}", [128, D], BF16, pctx) for i in range(3)]
        ln2_tile = [b.sbuf(f"ln2_tile{i}_h{h}", [128, D], BF16, pctx) for i in range(3)]
        pT_xn = [b.psum(f"pT_xn{i}_h{h}", [128, 1024], BF16, pctx) for i in range(3)]
        pT_x2 = [b.psum(f"pT_x2{i}_h{h}", [128, 1024], BF16, pctx) for i in range(3)]
        pfree = {}
        xn_busy = [None] * 3
        ln2_busy = [None] * 3
        vtoks = [None] * 3

        def front(tt):
            buf = tt % 3
            xt = x_tm[tt % 4]
            st_ = stats[buf]
            mv = mvcol[buf]
            xn_t = xn_tile[buf]
            ln2_t = ln2_tile[buf]
            t0 = t0h + tt * 128
            b.wait("sync", x_busy[tt % 4], xguard if tt < 4 else None)
            ld = b.SQ.issue(lambda e, xt=xt, t0=t0: e.dma_start(
                out=xt[:, :], in_=x_d[t0:t0 + 128, :]))
            b.wait("vector", ld)
            b.emit("vector", lambda e, xt=xt, st_=st_: e.bn_stats(st_[:, 0:6], xt[:, 0:512]))
            b.emit("vector", lambda e, xt=xt, st_=st_: e.bn_stats(st_[:, 6:12], xt[:, 512:1024]))
            b.emit("vector", lambda e, st_=st_, mv=mv: e.bn_aggr(mv[:, 0:2], st_[:, 0:12]))
            sv = b.tok("ve")
            b.wait("scalar", sv)
            b.emit("scalar", lambda e, mv=mv: e.activation(mv[:, 2:3], mv[:, 1:2], AF.Ln, bias=cc[:, 0:1]))
            b.emit("scalar", lambda e, mv=mv: e.activation(mv[:, 3:4], mv[:, 2:3], AF.Exp, scale=-0.5))
            b.emit("scalar", lambda e, mv=mv: e.activation(mv[:, 4:5], mv[:, 2:3], AF.Exp, scale=-1.0))
            b.emit("scalar", lambda e, mv=mv: e.activation(mv[:, 6:7], mv[:, 1:2], AF.Ln, bias=cc[:, 0:1], scale=mv[:, 4:5]))
            sc2 = b.emit("scalar", lambda e, mv=mv: e.activation(mv[:, 7:8], mv[:, 6:7], AF.Exp, scale=-0.5), inc="ac")
            b.wait("vector", xn_busy[buf], sc2)
            b.emit("vector", lambda e, xt=xt, xn_t=xn_t, mv=mv: e.tensor_scalar(
                xn_t[:, :], xt[:, :], mv[:, 0:1], mv[:, 3:4], ALU.subtract, ALU.mult))
            b.wait("vector", sc2, ln2_busy[buf])
            vtoks[buf] = b.emit("vector", lambda e, xn_t=xn_t, ln2_t=ln2_t, mv=mv: e.tensor_scalar(
                ln2_t[:, :], xn_t[:, :], mv[:, 7:8], None, ALU.mult), inc="ve")

        def back(tt):
            buf = tt % 3
            par = tt % 2
            xt = x_tm[tt % 4]
            xn_t = xn_tile[buf]
            ln2_t = ln2_tile[buf]
            vtok = vtoks[buf]
            pxn = pT_xn[buf]
            px2 = pT_x2[buf]
            b.wait("tensor", vtok, pfree.get(id(pxn)), pfree.get(id(px2)))
            for dc in range(NDC):
                tsrc = xn_t[:, dc * 128:(dc + 1) * 128]
                tdst = pxn[:, dc * 128:(dc + 1) * 128]
                b.emit("tensor", lambda e, tdst=tdst, tsrc=tsrc: e.transpose(tdst, tsrc, ident_b[:, :]))
            ptok1 = b.tok("pe")
            for dc in range(NDC):
                tsrc = ln2_t[:, dc * 128:(dc + 1) * 128]
                tdst = px2[:, dc * 128:(dc + 1) * 128]
                b.emit("tensor", lambda e, tdst=tdst, tsrc=tsrc: e.transpose(tdst, tsrc, ident_b[:, :]))
            ptok2 = b.tok("pe")
            xn_busy[buf] = ptok1
            ln2_busy[buf] = ptok2
            dxn = xn[:, :].rearrange("q (c t) -> q c t", t=TH)[:, :, tt * 128:(tt + 1) * 128]
            if par == 0:
                b.wait("vector", ptok1)
                etok1 = b.emit("vector", lambda e, dxn=dxn, pxn=pxn: e.tensor_copy(dxn, pxn[:, :]), inc="ve")
            else:
                b.wait("scalar", ptok1)
                etok1 = b.emit("scalar", lambda e, dxn=dxn, pxn=pxn: e.copy(dxn, pxn[:, :]), inc="ac")
            pfree[id(pxn)] = etok1
            if par == 0:
                b.wait("scalar", ptok2)
            else:
                b.wait("vector", ptok2)
            for dc in range(NDC):
                dst = xn2[:, dc * TH + tt * 128: dc * TH + (tt + 1) * 128]
                psrc = px2[:, dc * 128:(dc + 1) * 128]
                gcol = gam[:, dc:dc + 1]
                bcol = bet[:, dc:dc + 1]
                if par == 0:
                    b.emit("scalar", lambda e, dst=dst, psrc=psrc, gcol=gcol, bcol=bcol:
                           e.activation(dst, psrc, AF.Identity, bias=bcol, scale=gcol))
                else:
                    b.emit("vector", lambda e, dst=dst, psrc=psrc, gcol=gcol, bcol=bcol:
                           e.tensor_scalar(dst, psrc, gcol, bcol, ALU.mult, ALU.add))
            etok2 = b.tok("ac") if par == 0 else b.tok("ve")
            pfree[id(px2)] = etok2
            dsta = acc[:, :].rearrange("q (c t) -> q c t", t=TH)[:, :, tt * 128:(tt + 1) * 128]
            xadd = xn[:, :].rearrange("q (c t) -> q c t", t=TH)[:, :, tt * 128:(tt + 1) * 128]
            b.wait("vector", etok1, *acc_ld)
            ev3 = b.emit("vector", lambda e, dsta=dsta, xadd=xadd:
                         e.tensor_tensor(dsta, dsta, xadd, ALU.add), inc="ve")
            x_busy[tt % 4] = ptok1

        front(0)
        front(1)
        acc_ld.extend(issue_acc_ld())
        front(2)
        for tt in range(3, NT128):
            back(tt - 3)
            front(tt)
        back(NT128 - 3)
        back(NT128 - 2)
        back(NT128 - 1)


def _u_stage(b, nc, h, w1_d, w2_d, s1c, s2c, xn, h_sb, wub, guard):
    with ExitStack() as uctx:
        w1b, w2b = wub[0:3], wub[3:6]
        silu_t = [b.sbuf(f"silu{i}_h{h}", [128, 512], F32, uctx) for i in range(2)]
        pu = [b.psum(f"pu{i}_h{h}", [128, 512], F32, uctx) for i in range(8)]
        consumed = {}
        pu_free = {}
        silu_free = {}
        b.wait("gpsimd", guard)
        for oc in range(NHC):
            buf = oc % 3
            b.wait("gpsimd", consumed.get(buf))
            wt1 = b.GQ.issue(lambda e, oc=oc, buf=buf: e.dma_start(
                out=w1b[buf][:, :], in_=w1_d[oc].rearrange("p c o -> p (c o)")))
            wt2 = b.GQ.issue(lambda e, oc=oc, buf=buf: e.dma_start(
                out=w2b[buf][:, :], in_=w2_d[oc].rearrange("p c o -> p (c o)")))
            banks = [pu[(2 * oc) % 8], pu[(2 * oc + 1) % 8],
                     pu[(2 * oc + 4) % 8], pu[(2 * oc + 5) % 8]]
            b.wait("tensor", wt1, wt2, *[pu_free.get(id(bk)) for bk in banks])
            for wsel, wb in ((0, w1b[buf]), (1, w2b[buf])):
                for ic in range(NDC):
                    lhs = wb[:, ic * 128:(ic + 1) * 128]
                    for ti in range(NTT):
                        bank = banks[2 * wsel + ti]
                        rhs = xn[:, ic * TH + ti * 512: ic * TH + (ti + 1) * 512]
                        b.emit("tensor", lambda e, bank=bank, lhs=lhs, rhs=rhs,
                               st=(ic == 0), sp=(ic == NDC - 1):
                               e.matmul(bank[:, :], lhs, rhs, start=st, stop=sp))
            mm_tok = b.tok("pe")
            consumed[buf] = mm_tok
            for ti in range(NTT):
                st = silu_t[ti]
                scol = s1c[:, oc:oc + 1]
                s2col = s2c[:, oc:oc + 1]
                u1bank = banks[ti]
                u2bank = banks[2 + ti]
                b.wait("scalar", mm_tok, silu_free.get(id(st)))
                atok = b.emit("scalar", lambda e, st=st, u1bank=u1bank, scol=scol:
                              e.activation(st[:, :], u1bank[:, :], AF.Silu, scale=scol),
                              inc="ac")
                dst = h_sb[:, oc * TH + ti * 512: oc * TH + (ti + 1) * 512]
                b.wait("vector", atok)
                vtok = b.emit("vector", lambda e, dst=dst, u2bank=u2bank, s2col=s2col, st=st:
                              e.scalar_tensor_tensor(dst, u2bank[:, :], s2col, st[:, :],
                                                     ALU.mult, ALU.mult), inc="ve")
                silu_free[id(st)] = vtok
                pu_free[id(u1bank)] = atok
                pu_free[id(u2bank)] = vtok
        b.barrier()


def _attn_phase(b, nc, h, wq_d, wk_d, wv_d, wa_d, wo_d, ban, cc, e2, e2b, zeros64,
                mask2, ident_b, xn2, acc, S_sb, S_bf, tap):
    with ExitStack() as actx:
        v_tm = b.sbuf(f"v_tm_h{h}", [128, NT128 * DI], BF16, actx)
        o_sb = b.sbuf(f"o_sb_h{h}", [128, NIC * TH], BF16, actx)
        pa = [b.psum(f"pa{i}_h{h}", [128, 512], F32, actx) for i in range(6)]
        ptb = [b.psum(f"ptb{i}_h{h}", [128, 1024], BF16, actx) for i in range(2)]
        pa_free = {}
        ptb_free = {}

        def use_bank(bank, eng="tensor"):
            b.wait(eng, pa_free.get(id(bank)))

        PP = []
        for p in range(NIC):
            def qsb(name, shape, dt, p=p):
                return b.sbuf(f"{name}_q{p}_h{h}", shape, dt, actx)
            PP.append(dict(
                q=qsb("qt", [128, TH], BF16),
                ktm=qsb("ktm", [128, NCH * 128], BF16),
                pm=qsb("pm", [128, NCH * 2 * CHUNK], BF16),
                cend=qsb("cend", [128, NCH], F32),
            ))
        B2 = []
        for par in range(2):
            def psb(name, shape, dt, par=par):
                return b.sbuf(f"{name}_s{par}_h{h}", shape, dt, actx)
            B2.append(dict(
                w0=psb("w0", [128, NDC * 128], BF16),
                w1=psb("w1", [128, NDC * 128], BF16),
                k=psb("kp", [128, TH], BF16),
                tsc=psb("tscp", [128, TH], F32),
                cinv=psb("cinvp", [128, TH], BF16),
                k2=psb("k2p", [128, TH], BF16),
                sqn=psb("sqnp", [2, TH], F32),
                rn=psb("rnp", [2, TH], BF16),
            ))
        a_end = {}
        a_ktm = {}
        v_done = {}

        def stageA(p):
            yield
            par = p % 2
            Bp = B2[par]
            Pp = PP[p]
            Bp["kr"] = Bp["cinv"]
            bankA, bankB, bankC = pa[3 * par], pa[3 * par + 1], pa[3 * par + 2]
            tb = ptb[par]
            cbf = Pp["ktm"]
            if p - 2 in a_end:
                for eng in ("tensor", "vector", "scalar", "gpsimd"):
                    b.wait(eng, *a_end[p - 2])
            wa_t = b.GQ.issue(lambda e, dst=Bp["w0"], p=p: e.dma_start(
                out=dst[:, :], in_=wa_d[p].rearrange("q c o -> q (c o)")))
            wk_t = b.GQ.issue(lambda e, dst=Bp["w1"], p=p: e.dma_start(
                out=dst[:, :], in_=wk_d[p].rearrange("q c o -> q (c o)")))

            def proj(wsb, wtok, dst, bias_col, func):
                toks = []
                for ti in range(NTT):
                    bank = bankA if ti == 0 else bankB
                    use_bank(bank)
                    b.wait("tensor", wtok)
                    for ic in range(NDC):
                        lhs = wsb[:, ic * 128:(ic + 1) * 128]
                        rhs = xn2[:, ic * TH + ti * 512: ic * TH + (ti + 1) * 512]
                        b.emit("tensor", lambda e, bank=bank, lhs=lhs, rhs=rhs,
                               st=(ic == 0), sp=(ic == NDC - 1):
                               e.matmul(bank[:, :], lhs, rhs, start=st, stop=sp))
                    mt = b.tok("pe")
                    dsl = dst[:, ti * 512:(ti + 1) * 512]
                    b.wait("scalar", mt)
                    if func is None:
                        et = b.emit("scalar", lambda e, dsl=dsl, bank=bank:
                                    e.copy(dsl, bank[:, :]), inc="ac")
                    else:
                        et = b.emit("scalar", lambda e, dsl=dsl, bank=bank,
                                    func=func, bias_col=bias_col:
                                    e.activation(dsl, bank[:, :], func,
                                                 bias=bias_col, scale=-1.0), inc="ac")
                    pa_free[id(bank)] = et
                    toks.append(et)
                return toks

            ztoks = proj(Bp["w0"], wa_t, Bp["tsc"], ban[:, p:p + 1], AF.Exp)
            b.wait("scalar", *ztoks)
            sp_tok = b.emit("scalar", lambda e, tsc=Bp["tsc"]:
                            e.activation(tsc[:, :], tsc[:, :], AF.Ln, bias=1.0), inc="ac")
            yield
            ktoks = proj(Bp["w1"], wk_t, Bp["k"], None, None)
            b.wait("gpsimd", b.tok("pe"))
            wv_t = b.GQ.issue(lambda e, dst=Bp["w1"], p=p: e.dma_start(
                out=dst[:, :], in_=wv_d[p].rearrange("q c o -> q (c o)")))
            b.wait("vector", sp_tok)
            for n in range(NCH):
                sl = Bp["tsc"][:, n * CHUNK:(n + 1) * CHUNK]
                b.emit("vector", lambda e, sl=sl: e.tensor_tensor_scan(
                    sl, sl, zeros64[:, :], 0.0, ALU.add, ALU.add))
            scan_tok = b.tok("ve")
            yield
            b.wait("scalar", scan_tok)
            b.emit("scalar", lambda e, cbf=cbf, tsc=Bp["tsc"]: e.activation(
                cbf[:, :], tsc[:, :], AF.Exp, scale=-1.0))
            cend_src = Bp["tsc"][:, :].rearrange("q (n c) -> q n c", c=CHUNK)[:, :, CHUNK - 1]
            ce_tok = b.emit("scalar", lambda e, cend=Pp["cend"], cend_src=cend_src:
                            e.activation(cend[:, :], cend_src, AF.Exp, scale=-1.0),
                            inc="ac")
            b.wait("vector", ce_tok)
            cl_tok = b.emit("vector", lambda e, tsc=Bp["tsc"]: e.tensor_scalar(
                tsc[:, :], tsc[:, :], CLAMP, None, ALU.min), inc="ve")
            b.wait("scalar", cl_tok)
            ci_tok = b.emit("scalar", lambda e, cinv=Bp["cinv"], tsc=Bp["tsc"]:
                            e.activation(cinv[:, :], tsc[:, :], AF.Exp), inc="ac")
            yield
            b.wait("vector", *ktoks)
            k2tok = b.emit("vector", lambda e, k2=Bp["k2"], k_sb=Bp["k"]: e.tensor_tensor(
                k2[:, :], k_sb[:, :], k_sb[:, :], ALU.mult), inc="ve")
            b.wait("tensor", k2tok)
            sq_banks = [bankC, bankA]
            for ti in range(NTT):
                bank = sq_banks[ti]
                use_bank(bank)
                rhs = Bp["k2"][:, ti * 512:(ti + 1) * 512]
                b.emit("tensor", lambda e, bank=bank, rhs=rhs: e.matmul(
                    bank[0:2, 0:512], e2[:, :], rhs, start=True, stop=True))
            mt = b.tok("pe")
            b.wait("vector", mt)
            sq_tok = None
            for ti in range(NTT):
                bank = sq_banks[ti]
                dst = Bp["sqn"][:, ti * 512:(ti + 1) * 512]
                sq_tok = b.emit("vector", lambda e, dst=dst, bank=bank: e.tensor_copy(
                    dst, bank[0:2, 0:512]), inc="ve")
            pa_free[id(bankC)] = sq_tok
            pa_free[id(bankA)] = sq_tok
            b.wait("scalar", sq_tok)
            b.emit("scalar", lambda e, sqn=Bp["sqn"]: e.activation(
                sqn[:, :], sqn[:, :], AF.Ln, bias=cc[0:2, 1:2]))
            rn_tok = b.emit("scalar", lambda e, rn=Bp["rn"], sqn=Bp["sqn"]: e.activation(
                rn[:, :], sqn[:, :], AF.Exp, scale=-0.5), inc="ac")
            yield
            b.wait("tensor", rn_tok)
            kn_tok = None
            for ti in range(NTT):
                bank = sq_banks[ti]
                use_bank(bank)
                rhs = Bp["rn"][:, ti * 512:(ti + 1) * 512]
                b.emit("tensor", lambda e, bank=bank, rhs=rhs: e.matmul(
                    bank[:, :], e2b[:, :], rhs, start=True, stop=True))
                mt = b.tok("pe")
                ksl = Bp["k"][:, ti * 512:(ti + 1) * 512]
                b.wait("vector", mt)
                kn_tok = b.emit("vector", lambda e, ksl=ksl, bank=bank:
                                e.scalar_tensor_tensor(ksl, bank[:, :], 0.0, ksl,
                                                       ALU.bypass, ALU.mult), inc="ve")
                pa_free[id(bank)] = kn_tok
            yield
            wq_t = b.GQ.issue(lambda e, dst=Bp["w0"], p=p: e.dma_start(
                out=dst[:, :], in_=wq_d[p].rearrange("q c o -> q (c o)")))
            qtoks = proj(Bp["w0"], wq_t, Pp["q"], None, None)
            b.wait("vector", *qtoks, ci_tok)
            b.emit("vector", lambda e, q_sb=Pp["q"], cbf=cbf: e.tensor_tensor(
                q_sb[:, :], q_sb[:, :], cbf[:, :], ALU.mult))
            qk_tok = b.emit("vector", lambda e, k_sb=Bp["k"], cinv=Bp["cinv"]:
                            e.tensor_tensor(k_sb[:, :], k_sb[:, :], cinv[:, :], ALU.mult),
                            inc="ve")
            yield
            b.wait("vector", qk_tok)
            krts = []
            for n in range(NCH):
                krsl = Bp["kr"][:, n * CHUNK:(n + 1) * CHUNK]
                ksl = Bp["k"][:, n * CHUNK:(n + 1) * CHUNK]
                ccol = Pp["cend"][:, n:n + 1]
                krts.append(b.emit("vector", lambda e, krsl=krsl, ksl=ksl, ccol=ccol:
                            e.tensor_scalar(krsl, ksl, ccol, None, ALU.mult), inc="ve"))
            yield
            b.wait("tensor", wv_t)
            for t128 in range(NT128):
                bank = bankA if t128 % 2 == 0 else bankB
                use_bank(bank)
                for ic in range(NDC):
                    lhs = xn2[:, ic * TH + t128 * 128: ic * TH + (t128 + 1) * 128]
                    rhs = Bp["w1"][:, ic * 128:(ic + 1) * 128]
                    b.emit("tensor", lambda e, bank=bank, lhs=lhs, rhs=rhs,
                           st=(ic == 0), sp=(ic == NDC - 1):
                           e.matmul(bank[:, 0:128], lhs, rhs, start=st, stop=sp))
                mt = b.tok("pe")
                dst = v_tm[:, t128 * DI + p * 128:t128 * DI + (p + 1) * 128]
                b.wait("scalar", mt)
                et = b.emit("scalar", lambda e, dst=dst, bank=bank: e.copy(
                    dst, bank[:, 0:128]), inc="ac")
                pa_free[id(bank)] = et
                if t128 % 2 == 1:
                    yield
            v_done[p] = b.tok("ac")
            b.wait("tensor", qk_tok)
            for n in range(NCH):
                use_bank(bankA)
                use_bank(bankB)
                lhs0 = Bp["k"][0:64, n * CHUNK:(n + 1) * CHUNK]
                rhs0 = Pp["q"][0:64, n * CHUNK:(n + 1) * CHUNK]
                lhs1 = Bp["k"][64:128, n * CHUNK:(n + 1) * CHUNK]
                rhs1 = Pp["q"][64:128, n * CHUNK:(n + 1) * CHUNK]
                b.emit("tensor", lambda e, bankA=bankA, lhs0=lhs0, rhs0=rhs0:
                       e.matmul(bankA[:, 0:128], lhs0, rhs0, start=True, stop=True,
                                tile_position=(0, 0)))
                b.emit("tensor", lambda e, bankB=bankB, lhs1=lhs1, rhs1=rhs1:
                       e.matmul(bankB[:, 0:128], lhs1, rhs1, start=True, stop=True,
                                tile_position=(64, 0)))
                pp_tok = b.tok("pe")
                b.wait("vector", pp_tok)
                pd0 = Pp["pm"][:, n * 256: n * 256 + 128]
                pd1 = Pp["pm"][:, n * 256 + 128: (n + 1) * 256]
                b.emit("vector", lambda e, pd0=pd0, bankA=bankA:
                       e.tensor_tensor(pd0, bankA[:, 0:128], mask2[:, :], ALU.mult))
                ev2 = b.emit("vector", lambda e, pd1=pd1, bankB=bankB:
                             e.tensor_tensor(pd1, bankB[:, 0:128], mask2[:, :], ALU.mult),
                             inc="ve")
                pa_free[id(bankA)] = ev2
                pa_free[id(bankB)] = ev2
                if n % 2 == 1:
                    yield
            b.wait("tensor", krts[-1], ptb_free.get(id(tb)))
            for n in range(NCH):
                tdst = tb[:, n * 128:(n + 1) * 128]
                krsl = Bp["kr"][:, n * CHUNK:(n + 1) * CHUNK]
                b.emit("tensor", lambda e, tdst=tdst, krsl=krsl:
                       e.transpose(tdst, krsl, ident_b[:, :]))
            ttk = b.tok("pe")
            b.wait("scalar", ttk)
            ev1 = b.emit("scalar", lambda e, kdst=Pp["ktm"], tb=tb:
                         e.copy(kdst[:, :], tb[:, :]), inc="ac")
            ptb_free[id(tb)] = ev1
            a_ktm[p] = ev1
            a_end[p] = (b.tok("pe"), b.tok("ve"), b.tok("ac"))

        def drain(g):
            if g is not None:
                for _ in g:
                    pass

        def interleave(g1, g2):
            while True:
                s1 = next(g1, StopIteration) is StopIteration if g1 else True
                s2 = next(g2, StopIteration) is StopIteration if g2 else True
                if s1 and s2:
                    break

        a_gens = [stageA(p) for p in range(NIC)]
        interleave(a_gens[0], a_gens[1])
        interleave(a_gens[2], a_gens[3])

        sbf_tok = {p: None for p in range(NIC)}
        for n in range(NCH):
            toff = n * DI
            for p in range(NIC):
                Pp = PP[p]
                Sbfp = S_bf[p]
                kv_bank = pa[p % 3]
                o_bank = pa[3 + p % 3]
                use_bank(kv_bank)
                b.wait("tensor", a_ktm.get(p), v_done.get(p))
                klhs = Pp["ktm"][:, n * 128:(n + 1) * 128]
                vrhs = v_tm[:, toff + p * 128: toff + (p + 1) * 128]
                b.emit("tensor", lambda e, kv_bank=kv_bank, klhs=klhs, vrhs=vrhs:
                       e.matmul(kv_bank[:, 0:128], klhs, vrhs, start=True, stop=True))
                kv_tok = b.tok("pe")
                use_bank(o_bank)
                b.wait("tensor", sbf_tok[p])
                qsl = Pp["q"][:, n * CHUNK:(n + 1) * CHUNK]
                v0 = v_tm[:, toff + p * 128: toff + p * 128 + 64]
                v1 = v_tm[:, toff + p * 128 + 64: toff + (p + 1) * 128]
                pm0 = Pp["pm"][:, n * 256: n * 256 + 128]
                pm1 = Pp["pm"][:, n * 256 + 128: (n + 1) * 256]
                b.emit("tensor", lambda e, o_bank=o_bank, v0=v0, pm0=pm0:
                       e.matmul(o_bank[0:64, 0:CHUNK], v0, pm0, start=True, stop=False,
                                tile_position=(0, 0), skip_group_check=True))
                b.emit("tensor", lambda e, o_bank=o_bank, v1=v1, pm1=pm1:
                       e.matmul(o_bank[64:128, 0:CHUNK], v1, pm1, start=True, stop=False,
                                tile_position=(0, 64), skip_group_check=True))
                b.emit("tensor", lambda e, o_bank=o_bank, Sbfp=Sbfp, qsl=qsl:
                       e.matmul(o_bank[:, 0:CHUNK], Sbfp[:, :], qsl,
                                start=False, stop=True, skip_group_check=True))
                o_tok = b.tok("pe")
                ccol = Pp["cend"][:, n:n + 1]
                b.wait("vector", kv_tok, o_tok)
                b.emit("vector", lambda e, Sbfp=Sbfp, kv_bank=kv_bank, ccol=ccol:
                       e.scalar_tensor_tensor(Sbfp[0:64, 0:64], Sbfp[0:64, 0:64],
                                              ccol[0:64, :], kv_bank[0:64, 0:64],
                                              ALU.mult, ALU.add))
                kv_done = b.emit("vector", lambda e, Sbfp=Sbfp, kv_bank=kv_bank, ccol=ccol:
                       e.scalar_tensor_tensor(Sbfp[64:128, 64:128], Sbfp[64:128, 64:128],
                                              ccol[64:128, :], kv_bank[64:128, 64:128],
                                              ALU.mult, ALU.add), inc="ve")
                sbf_tok[p] = kv_done
                pa_free[id(kv_bank)] = kv_done
                odst = o_sb[:, p * TH + n * CHUNK: p * TH + (n + 1) * CHUNK]
                b.wait("scalar", o_tok)
                oev = b.emit("scalar", lambda e, odst=odst, o_bank=o_bank:
                             e.copy(odst, o_bank[:, 0:CHUNK]), inc="ac")
                pa_free[id(o_bank)] = oev

        b.barrier(engs=("tensor", "vector", "scalar"))
        tap(f"o_h{h}", o_sb[:, :], b.tok("ve"), b.tok("ac"))

        wo_bufs = [B2[0]["w0"], B2[0]["w1"], B2[1]["w0"], B2[1]["w1"]]
        for eng in ("gpsimd",):
            b.wait(eng, *a_end.get(2, ()), *a_end.get(3, ()))
        wo_toks = []
        for ic in range(NIC):
            wo_toks.append(b.GQ.issue(lambda e, dst=wo_bufs[ic], ic=ic: e.dma_start(
                out=dst[:, :], in_=wo_d[:, ic, :])))
        b.wait("tensor", *wo_toks)
        for dc in range(NDC):
            for ti in range(NTT):
                bank = pa[(dc * NTT + ti) % 4]
                use_bank(bank)
                for ic in range(NIC):
                    lhs = wo_bufs[ic][:, dc * 128:(dc + 1) * 128]
                    rhs = o_sb[:, ic * TH + ti * 512: ic * TH + (ti + 1) * 512]
                    b.emit("tensor", lambda e, bank=bank, lhs=lhs, rhs=rhs,
                           st=(ic == 0), sp=(ic == NIC - 1):
                           e.matmul(bank[:, :], lhs, rhs, start=st, stop=sp))
                mt = b.tok("pe")
                asl = acc[:, dc * TH + ti * 512: dc * TH + (ti + 1) * 512]
                b.wait("vector", mt)
                ev = b.emit("vector", lambda e, asl=asl, bank=bank:
                            e.scalar_tensor_tensor(asl, bank[:, :], 0.0, asl,
                                                   ALU.bypass, ALU.add), inc="ve")
                pa_free[id(bank)] = ev
        b.barrier(engs=("tensor", "vector", "scalar"))


def _ffn_tail(b, nc, h, h1_d, h2_d, w3_d, hs1c, hs2c, s3c, h_sb, acc, tap,
              t0h, out_d):
    store_toks = [None] * NDC
    with ExitStack() as fctx:
        g1 = b.sbuf(f"g1_h{h}", [128, NHC * TH], BF16, fctx)
        hw = [b.sbuf(f"hw{i}_h{h}", [128, NHC * 128], BF16, fctx) for i in range(3)]
        ph = [b.psum(f"ph{i}_h{h}", [128, 512], F32, fctx) for i in range(8)]
        ph_free = {}
        wconsumed = {}
        b.dma_gate("gpsimd")

        def h_stage(wd, nic_chunks, src, dst, dst_nch, scale_sb, mode):
            for oc in range(dst_nch):
                buf = oc % 3
                b.wait("gpsimd", wconsumed.get(buf))
                wt = b.GQ.issue(lambda e, oc=oc, buf=buf, wd=wd, nic_chunks=nic_chunks:
                                e.dma_start(out=hw[buf][:, 0:nic_chunks * 128],
                                            in_=wd[oc].rearrange("p c o -> p (c o)")))
                b.wait("tensor", wt)
                ets = []
                for ti in range(NTT):
                    bank = ph[(NTT * oc + ti) % 8]
                    b.wait("tensor", ph_free.get(id(bank)))
                    for ic in range(nic_chunks):
                        lhs = hw[buf][:, ic * 128:(ic + 1) * 128]
                        rhs = src[:, ic * TH + ti * 512: ic * TH + (ti + 1) * 512]
                        b.emit("tensor", lambda e, bank=bank, lhs=lhs, rhs=rhs,
                               st=(ic == 0), sp=(ic == nic_chunks - 1):
                               e.matmul(bank[:, :], lhs, rhs, start=st, stop=sp))
                    mt = b.tok("pe")
                    wconsumed[buf] = mt
                    scol = scale_sb[:, oc:oc + 1]
                    if mode == "gelu":
                        ddst = dst[:, oc * TH + ti * 512: oc * TH + (ti + 1) * 512]
                        b.wait("scalar", mt)
                        et = b.emit("scalar", lambda e, ddst=ddst, bank=bank, scol=scol:
                                    e.activation(ddst, bank[:, :], AF.Gelu, scale=scol),
                                    inc="ac")
                    else:
                        asl = acc[:, oc * TH + ti * 512: oc * TH + (ti + 1) * 512]
                        b.wait("vector", mt)
                        et = b.emit("vector", lambda e, asl=asl, bank=bank, scol=scol:
                                    e.scalar_tensor_tensor(asl, bank[:, :], scol, asl,
                                                           ALU.mult, ALU.add), inc="ve")
                    ph_free[id(bank)] = et
                    ets.append(et)
                if mode == "w3":
                    srcp = acc[:, oc * TH:(oc + 1) * TH]
                    dstp = out_d[oc * 128:(oc + 1) * 128, t0h:t0h + TH]
                    b.wait("sync", *ets)
                    store_toks[oc] = b.ST.issue(
                        lambda e, dstp=dstp, srcp=srcp: e.dma_start(out=dstp, in_=srcp))

        h_stage(h1_d, NHC, h_sb, g1, NHC, hs1c, "gelu")
        b.barrier(engs=("tensor", "vector", "scalar"))
        tap(f"g1_h{h}", g1[:, :], b.tok("ac"))
        h_stage(h2_d, NHC, g1, h_sb, NHC, hs2c, "gelu")
        b.barrier(engs=("tensor", "vector", "scalar"))
        b.h2_done = [b.tok("pe"), b.tok("ac"), b.tok("ve")]
        h_stage(w3_d, NHC, h_sb, None, NDC, s3c, "w3")
        b.barrier(engs=("tensor", "vector", "scalar"))
    return store_toks


def _prep_weights(inputs):
    w = {}
    w["wq"] = np.ascontiguousarray(np.asarray(inputs["L0_attn_Wq"]).reshape(NDC, 128, NIC, 128).transpose(2, 1, 0, 3))
    w["wk"] = np.ascontiguousarray(np.asarray(inputs["L0_attn_Wk"]).reshape(NDC, 128, NIC, 128).transpose(2, 1, 0, 3))
    w["wv"] = np.ascontiguousarray(np.asarray(inputs["L0_attn_Wv"]).reshape(NDC, 128, NIC, 128).transpose(2, 1, 0, 3))
    w["wa"] = np.ascontiguousarray(np.asarray(inputs["L0_attn_Wa"]).reshape(NDC, 128, NIC, 128).transpose(2, 1, 0, 3))
    w["wo"] = np.ascontiguousarray(np.asarray(inputs["L0_attn_Wo"]).reshape(NIC, 128, D).transpose(1, 0, 2))
    w["ba"] = np.ascontiguousarray(np.asarray(inputs["L0_attn_ba"]))
    w["gamma"] = np.ascontiguousarray(np.asarray(inputs["L0_attn_gamma"]))
    w["beta"] = np.ascontiguousarray(np.asarray(inputs["L0_attn_beta"]))
    w["w1t"] = np.ascontiguousarray(np.asarray(inputs["L0_ffn_w1_w"]).T.reshape(NDC, 128, NHC, 128).transpose(2, 1, 0, 3))
    w["w2t"] = np.ascontiguousarray(np.asarray(inputs["L0_ffn_w2_w"]).T.reshape(NDC, 128, NHC, 128).transpose(2, 1, 0, 3))
    w["h1t"] = np.ascontiguousarray(np.asarray(inputs["L0_ffn_h1_w"]).T.reshape(NHC, 128, NHC, 128).transpose(2, 1, 0, 3))
    w["h2t"] = np.ascontiguousarray(np.asarray(inputs["L0_ffn_h2_w"]).T.reshape(NHC, 128, NHC, 128).transpose(2, 1, 0, 3))
    w["w3t"] = np.ascontiguousarray(np.asarray(inputs["L0_ffn_w3_w"]).T.reshape(NHC, 128, NDC, 128).transpose(2, 1, 0, 3))
    w["s1"] = np.ascontiguousarray(np.asarray(inputs["L0_ffn_w1_s"]))
    w["s2"] = np.ascontiguousarray(np.asarray(inputs["L0_ffn_w2_s"]))
    w["hs1"] = np.ascontiguousarray(np.asarray(inputs["L0_ffn_h1_s"]))
    w["hs2"] = np.ascontiguousarray(np.asarray(inputs["L0_ffn_h2_s"]))
    w["s3"] = np.ascontiguousarray(np.asarray(inputs["L0_ffn_w3_s"]))
    return w


def kernel(**inputs):
    x = np.asarray(inputs["L0_attn_x"])
    w = _prep_weights(inputs)
    nc, _ = build_program()
    in_maps = []
    for c in range(NCORES):
        m = {"x": np.ascontiguousarray(x[c]),
             "xf": np.ascontiguousarray(x[c].T)}
        m.update(w)
        in_maps.append(m)
    res = run_bass_kernel_spmd(nc, in_maps, core_ids=list(range(NCORES)))
    out = np.stack([r["out"].T for r in res.results], axis=0)
    return np.ascontiguousarray(out).astype(np.float32)


# version 31 (final)
# speedup vs baseline: 1.0896x; 1.0896x over previous
import numpy as np
from contextlib import ExitStack

import concourse.bass as bass
import concourse.mybir as mybir
from concourse.bass_utils import run_bass_kernel_spmd

F32 = mybir.dt.float32
BF16 = mybir.dt.bfloat16
AF = mybir.ActivationFunctionType
ALU = mybir.AluOpType

NCORES = 8
D = 1024
DI = 512
DH = 4096
T = 2048
TH = T // 2
NT128 = TH // 128
NTT = TH // 512
NDC = D // 128
NIC = DI // 128
NHC = DH // 128
CHUNK = 128
NCH = TH // CHUNK
LN_EPS = 1e-5
CLAMP = 87.0

ENGINES = ("sync", "gpsimd", "tensor", "vector", "scalar")


SELF_SEM = {"tensor": "pe", "vector": "ve", "scalar": "ac", "gpsimd": "gp"}


class B:

    def __init__(self, nc, ctx):
        self.nc = nc
        self.ctx = ctx
        self.ops = {e: [] for e in ENGINES}
        self.sems = {}
        self.cnt = {}
        self.lastwait = {e: {} for e in ENGINES}
        self.bar_toks = []

    def sem(self, name):
        if name not in self.sems:
            self.sems[name] = self.ctx.enter_context(self.nc.semaphore(name))
            self.cnt[name] = 0
        return self.sems[name]

    def wait(self, eng, *toks):
        flat = []
        for t in toks:
            if isinstance(t, list):
                flat.extend(t)
            else:
                flat.append(t)
        for t in flat:
            if t is None:
                continue
            (s, v) = t
            if v <= 0 or self.lastwait[eng].get(s, -1) >= v:
                continue
            self.lastwait[eng][s] = v
            sem = self.sem(s)
            self.ops[eng].append(lambda e, sem=sem, v=v: e.wait_ge(sem, v))

    def emit(self, eng, fn, inc=None, by=1):
        if inc is None and eng in SELF_SEM:
            inc = SELF_SEM[eng]
        if eng in SELF_SEM and eng != "tensor":
            self.wait(eng, self.tok(SELF_SEM[eng]))
        if inc is None:
            self.ops[eng].append(fn)
            return None
        sem = self.sem(inc)
        self.cnt[inc] += by
        tok = (inc, self.cnt[inc])
        def wrapped(e, fn=fn, sem=sem, by=by):
            fn(e).then_inc(sem, by)
        self.ops[eng].append(wrapped)
        return tok

    def tok(self, s):
        if s in SELF_SEM:
            s = SELF_SEM[s]
        return (s, self.cnt.get(s, 0))

    def barrier(self, engs=("tensor", "vector", "scalar", "gpsimd")):
        toks = [self.tok(SELF_SEM[e]) for e in engs]
        for e in engs:
            self.wait(e, *toks)
        self.bar_toks = toks
        return toks

    def dma_gate(self, eng):
        self.wait(eng, *self.bar_toks)

    def sbuf(self, name, shape, dt, ctx=None):
        return (ctx or self.ctx).enter_context(self.nc.sbuf_tensor(name, shape, dt))

    def psum(self, name, shape, dt, ctx=None):
        return (ctx or self.ctx).enter_context(self.nc.psum_tensor(name, shape, dt))

    def finalize(self, block):
        for e in ENGINES:
            ops = self.ops[e]
            getattr(block, e)(lambda eng, ops=ops: [f(eng) for f in ops] and None)


class DmaQ:

    def __init__(self, b, eng, prefix, k):
        self.b = b
        self.eng = eng
        self.names = [f"{prefix}{i}" for i in range(k)]
        for s in self.names:
            b.sem(s)

    def issue(self, fn):
        s = min(self.names, key=lambda n: self.b.cnt[n])
        self.b.wait(self.eng, self.b.tok(s))
        return self.b.emit(self.eng, fn, inc=s, by=16)

    def all_done_toks(self):
        return [self.b.tok(s) for s in self.names]


def build_program(debug_taps=()):
    nc = bass.Bass("TRN2", target_bir_lowering=False)

    x_d = nc.dram_tensor("x", [T, D], F32, kind="ExternalInput").ap()
    xf_d = nc.dram_tensor("xf", [D, T], F32, kind="ExternalInput").ap()
    wq_d = nc.dram_tensor("wq", [NIC, 128, NDC, 128], F32, kind="ExternalInput").ap()
    wk_d = nc.dram_tensor("wk", [NIC, 128, NDC, 128], F32, kind="ExternalInput").ap()
    wv_d = nc.dram_tensor("wv", [NIC, 128, NDC, 128], F32, kind="ExternalInput").ap()
    wa_d = nc.dram_tensor("wa", [NIC, 128, NDC, 128], F32, kind="ExternalInput").ap()
    wo_d = nc.dram_tensor("wo", [128, NIC, D], F32, kind="ExternalInput").ap()
    ba_d = nc.dram_tensor("ba", [DI], F32, kind="ExternalInput").ap()
    gamma_d = nc.dram_tensor("gamma", [D], F32, kind="ExternalInput").ap()
    beta_d = nc.dram_tensor("beta", [D], F32, kind="ExternalInput").ap()
    w1_d = nc.dram_tensor("w1t", [NHC, 128, NDC, 128], F32, kind="ExternalInput").ap()
    w2_d = nc.dram_tensor("w2t", [NHC, 128, NDC, 128], F32, kind="ExternalInput").ap()
    h1_d = nc.dram_tensor("h1t", [NHC, 128, NHC, 128], F32, kind="ExternalInput").ap()
    h2_d = nc.dram_tensor("h2t", [NHC, 128, NHC, 128], F32, kind="ExternalInput").ap()
    w3_d = nc.dram_tensor("w3t", [NDC, 128, NHC, 128], F32, kind="ExternalInput").ap()
    s1_d = nc.dram_tensor("s1", [DH], F32, kind="ExternalInput").ap()
    s2_d = nc.dram_tensor("s2", [DH], F32, kind="ExternalInput").ap()
    hs1_d = nc.dram_tensor("hs1", [DH], F32, kind="ExternalInput").ap()
    hs2_d = nc.dram_tensor("hs2", [DH], F32, kind="ExternalInput").ap()
    s3_d = nc.dram_tensor("s3", [D], F32, kind="ExternalInput").ap()
    out_d = nc.dram_tensor("out", [D, T], F32, kind="ExternalOutput").ap()

    taps = {}

    with ExitStack() as ctx:
        b = B(nc, ctx)
        for s in ("pe", "ve", "ac", "gp"):
            b.sem(s)
        b.SQ = DmaQ(b, "sync", "sq", 4)
        b.GQ = DmaQ(b, "gpsimd", "gq", 6)
        b.ST = DmaQ(b, "sync", "st", 2)

        def tap(name, ap, *wait_toks):
            if name not in debug_taps:
                return
            td = nc.dram_tensor("tap_" + name, list(ap.shape), ap.dtype,
                                kind="ExternalOutput").ap()
            taps[name] = td
            b.wait("sync", *wait_toks)
            tk = b.ST.issue(lambda e, td=td, ap=ap: e.dma_start(out=td, in_=ap))
            for eng in ("tensor", "vector", "scalar", "gpsimd"):
                b.wait(eng, tk)

        ident_f = b.sbuf("ident_f", [128, 128], F32)
        ident_b = b.sbuf("ident_b", [128, 128], BF16)
        mask2 = b.sbuf("mask2", [128, 128], F32)
        e2 = b.sbuf("e2", [128, 2], BF16)
        e2b = b.sbuf("e2b", [2, 128], BF16)
        zeros64 = b.sbuf("zeros64", [128, CHUNK], F32)
        ban = b.sbuf("ban", [128, NIC], F32)
        gam = b.sbuf("gam", [128, NDC], F32)
        bet = b.sbuf("bet", [128, NDC], F32)
        s1c = b.sbuf("s1c", [128, NHC], F32)
        s2c = b.sbuf("s2c", [128, NHC], F32)
        hs1c = b.sbuf("hs1c", [128, NHC], F32)
        hs2c = b.sbuf("hs2c", [128, NHC], F32)
        s3c = b.sbuf("s3c", [128, NDC], F32)
        S_sb = [b.sbuf(f"S{p}", [128, 128], F32) for p in range(NIC)]
        S_bf = [b.sbuf(f"Sbf{p}", [128, 128], BF16) for p in range(NIC)]
        stats = [b.sbuf(f"stats{i}", [128, 16], F32) for i in range(3)]
        mvcol = [b.sbuf(f"mvcol{i}", [128, 8], F32) for i in range(3)]
        cc = b.sbuf("cc", [128, 2], F32)

        vec_toks = []
        for dst, src in ((ban, ba_d), (gam, gamma_d), (bet, beta_d),
                         (s1c, s1_d), (s2c, s2_d), (hs1c, hs1_d),
                         (hs2c, hs2_d), (s3c, s3_d)):
            vec_toks.append(b.SQ.issue(lambda e, dst=dst, src=src: e.dma_start(
                out=dst[:, :], in_=src.rearrange("(c p) -> p c", p=128),
                allow_slow_non_contiguous=True)))

        def g_memset(ap, v):
            b.wait("gpsimd", b.tok("gp"))
            b.emit("gpsimd", lambda e, ap=ap, v=v: e.memset(ap, v), inc="gp")

        def g_chain(fn):
            b.wait("gpsimd", b.tok("gp"))
            return b.emit("gpsimd", fn, inc="gp")
        g_memset(zeros64[:, :], 0.0)
        g_memset(cc[:, 0:1], LN_EPS)
        g_memset(cc[:, 1:2], 1e-24)
        g_memset(ident_f[:, :], 0.0)
        g_chain(lambda e: e.affine_select(
            out=ident_f[:, :], in_=ident_f[:, :], compare_op=ALU.not_equal,
            fill=1.0, base=0, pattern=[[-1, 128]], channel_multiplier=1))
        g_memset(ident_b[:, :], 0.0)
        g_chain(lambda e: e.affine_select(
            out=ident_b[:, :], in_=ident_b[:, :], compare_op=ALU.not_equal,
            fill=1.0, base=0, pattern=[[-1, 128]], channel_multiplier=1))
        g_memset(mask2[:, :], 1.0)
        g_chain(lambda e: e.affine_select(
            out=mask2[:, :], in_=mask2[:, :], compare_op=ALU.is_ge,
            fill=0.0, base=0, pattern=[[1, 128]], channel_multiplier=-1))
        g_memset(e2[:, :], 0.0)
        g_memset(e2[0:64, 0:1], 1.0)
        g_memset(e2[64:128, 1:2], 1.0)
        g_memset(e2b[:, :], 1.0)
        g_chain(lambda e: e.affine_select(
            out=e2b[:, :], in_=e2b[:, :], compare_op=ALU.is_ge,
            fill=0.0, base=0, pattern=[[1, 128]], channel_multiplier=-64))
        g_chain(lambda e: e.affine_select(
            out=e2b[:, :], in_=e2b[:, :], compare_op=ALU.is_ge,
            fill=0.0, base=63, pattern=[[-1, 128]], channel_multiplier=64))
        for p in range(NIC):
            g_memset(S_sb[p][:, :], 0.0)
            g_memset(S_bf[p][:, :], 0.0)
        gp_tok = b.tok("gp")
        b.wait("vector", *vec_toks, gp_tok)
        b.emit("vector", lambda e: e.tensor_scalar(
            ban[:, :], ban[:, :], -1.0, None, ALU.mult), inc="ve")
        b.barrier()


        prev_store_toks = [None] * NDC
        for h in range(2):
            t0h = h * TH
            with ExitStack() as hctx:
                acc = b.sbuf(f"acc_h{h}", [128, NDC * TH], F32, hctx)
                h_sb = b.sbuf(f"hbuf_h{h}", [128, NHC * TH], BF16, hctx)
                def issue_acc_ld(acc=acc, t0h=t0h, pst=prev_store_toks):
                    toks = []
                    for dc in range(NDC):
                        dst = acc[:, dc * TH:(dc + 1) * TH]
                        fsrc = xf_d[dc * 128:(dc + 1) * 128, t0h:t0h + TH]
                        b.wait("sync", pst[dc])
                        toks.append(b.SQ.issue(lambda e, dst=dst, fsrc=fsrc:
                                               e.dma_start(out=dst, in_=fsrc)))
                    return toks

                with ExitStack() as xctx:
                    xn2 = b.sbuf(f"xn2_h{h}", [128, NDC * TH], BF16, xctx)

                    with ExitStack() as lctx:
                        xn = b.sbuf(f"xn_h{h}", [128, NDC * TH], BF16, lctx)
                        wub = [b.sbuf(f"wub{i}_h{h}", [128, NDC * 128], BF16,
                                      lctx) for i in range(6)]
                        _ln_phase(b, nc, h, t0h, x_d, stats, mvcol, cc,
                                  ident_f, ident_b, gam, bet,
                                  xn, xn2, acc, lctx, issue_acc_ld,
                                  getattr(b, "h2_done", None) if h else None)
                        b.barrier()
                        tap(f"xn_h{h}", xn[:, :], b.tok("ac"), b.tok("ve"))
                        tap(f"xn2_h{h}", xn2[:, :], b.tok("ve"))
                        tap(f"acc_ln_h{h}", acc[:, :], b.tok("ve"))

                        _u_stage(b, nc, h, w1_d, w2_d, s1c, s2c, xn, h_sb, wub,
                                 getattr(b, "h2_done", None) if h else None)
                        b.barrier()
                        tap(f"hbuf_h{h}", h_sb[:, :], b.tok("ve"))

                    _attn_phase(b, nc, h, wq_d, wk_d, wv_d, wa_d, wo_d, ban, cc,
                                e2, e2b, zeros64, mask2, ident_b, xn2, acc,
                                S_sb, S_bf, tap)
                    b.barrier()
                    tap(f"acc_attn_h{h}", acc[:, :], b.tok("ve"))

                prev_store_toks = _ffn_tail(b, nc, h, h1_d, h2_d, w3_d,
                                            hs1c, hs2c, s3c, h_sb, acc, tap,
                                            t0h, out_d)
                tap(f"acc_out_h{h}", acc[:, :], b.tok("ve"))

        b.wait("sync", *b.ST.all_done_toks())

        block = ctx.enter_context(nc.Block())
        b.finalize(block)

    return nc, taps


def _ln_phase(b, nc, h, t0h, x_d, stats, mvcol, cc,
              ident_f, ident_b, gam, bet, xn, xn2, acc, lctx, issue_acc_ld,
              xguard):
    acc_ld = []
    if xguard:
        b.wait("vector", xguard)
        b.wait("scalar", xguard)
    with ExitStack() as pctx:
        x_tm = [b.sbuf(f"xin{i}_h{h}", [128, D], F32, pctx) for i in range(4)]
        x_busy = [None] * 4
        xn_tile = [b.sbuf(f"xn_tile{i}_h{h}", [128, D], BF16, pctx) for i in range(3)]
        ln2_tile = [b.sbuf(f"ln2_tile{i}_h{h}", [128, D], BF16, pctx) for i in range(3)]
        pT_xn = [b.psum(f"pT_xn{i}_h{h}", [128, 1024], BF16, pctx) for i in range(3)]
        pT_x2 = [b.psum(f"pT_x2{i}_h{h}", [128, 1024], BF16, pctx) for i in range(3)]
        pfree = {}
        xn_busy = [None] * 3
        ln2_busy = [None] * 3
        vtoks = [None] * 3

        def front(tt):
            buf = tt % 3
            xt = x_tm[tt % 4]
            st_ = stats[buf]
            mv = mvcol[buf]
            xn_t = xn_tile[buf]
            ln2_t = ln2_tile[buf]
            t0 = t0h + tt * 128
            b.wait("sync", x_busy[tt % 4], xguard if tt < 4 else None)
            ld = b.SQ.issue(lambda e, xt=xt, t0=t0: e.dma_start(
                out=xt[:, :], in_=x_d[t0:t0 + 128, :]))
            b.wait("vector", ld)
            b.emit("vector", lambda e, xt=xt, st_=st_: e.bn_stats(st_[:, 0:6], xt[:, 0:512]))
            b.emit("vector", lambda e, xt=xt, st_=st_: e.bn_stats(st_[:, 6:12], xt[:, 512:1024]))
            b.emit("vector", lambda e, st_=st_, mv=mv: e.bn_aggr(mv[:, 0:2], st_[:, 0:12]))
            sv = b.tok("ve")
            b.wait("scalar", sv)
            b.emit("scalar", lambda e, mv=mv: e.activation(mv[:, 2:3], mv[:, 1:2], AF.Ln, bias=cc[:, 0:1]))
            b.emit("scalar", lambda e, mv=mv: e.activation(mv[:, 3:4], mv[:, 2:3], AF.Exp, scale=-0.5))
            b.emit("scalar", lambda e, mv=mv: e.activation(mv[:, 4:5], mv[:, 2:3], AF.Exp, scale=-1.0))
            b.emit("scalar", lambda e, mv=mv: e.activation(mv[:, 6:7], mv[:, 1:2], AF.Ln, bias=cc[:, 0:1], scale=mv[:, 4:5]))
            sc2 = b.emit("scalar", lambda e, mv=mv: e.activation(mv[:, 7:8], mv[:, 6:7], AF.Exp, scale=-0.5), inc="ac")
            b.wait("vector", xn_busy[buf], sc2)
            b.emit("vector", lambda e, xt=xt, xn_t=xn_t, mv=mv: e.tensor_scalar(
                xn_t[:, :], xt[:, :], mv[:, 0:1], mv[:, 3:4], ALU.subtract, ALU.mult))
            b.wait("vector", sc2, ln2_busy[buf])
            vtoks[buf] = b.emit("vector", lambda e, xn_t=xn_t, ln2_t=ln2_t, mv=mv: e.tensor_scalar(
                ln2_t[:, :], xn_t[:, :], mv[:, 7:8], None, ALU.mult), inc="ve")

        def back(tt):
            buf = tt % 3
            par = tt % 2
            xt = x_tm[tt % 4]
            xn_t = xn_tile[buf]
            ln2_t = ln2_tile[buf]
            vtok = vtoks[buf]
            pxn = pT_xn[buf]
            px2 = pT_x2[buf]
            b.wait("tensor", vtok, pfree.get(id(pxn)), pfree.get(id(px2)))
            for dc in range(NDC):
                tsrc = xn_t[:, dc * 128:(dc + 1) * 128]
                tdst = pxn[:, dc * 128:(dc + 1) * 128]
                b.emit("tensor", lambda e, tdst=tdst, tsrc=tsrc: e.transpose(tdst, tsrc, ident_b[:, :]))
            ptok1 = b.tok("pe")
            for dc in range(NDC):
                tsrc = ln2_t[:, dc * 128:(dc + 1) * 128]
                tdst = px2[:, dc * 128:(dc + 1) * 128]
                b.emit("tensor", lambda e, tdst=tdst, tsrc=tsrc: e.transpose(tdst, tsrc, ident_b[:, :]))
            ptok2 = b.tok("pe")
            xn_busy[buf] = ptok1
            ln2_busy[buf] = ptok2
            dxn = xn[:, :].rearrange("q (c t) -> q c t", t=TH)[:, :, tt * 128:(tt + 1) * 128]
            if par == 0:
                b.wait("vector", ptok1)
                etok1 = b.emit("vector", lambda e, dxn=dxn, pxn=pxn: e.tensor_copy(dxn, pxn[:, :]), inc="ve")
            else:
                b.wait("scalar", ptok1)
                etok1 = b.emit("scalar", lambda e, dxn=dxn, pxn=pxn: e.copy(dxn, pxn[:, :]), inc="ac")
            pfree[id(pxn)] = etok1
            if par == 0:
                b.wait("scalar", ptok2)
            else:
                b.wait("vector", ptok2)
            for dc in range(NDC):
                dst = xn2[:, dc * TH + tt * 128: dc * TH + (tt + 1) * 128]
                psrc = px2[:, dc * 128:(dc + 1) * 128]
                gcol = gam[:, dc:dc + 1]
                bcol = bet[:, dc:dc + 1]
                if par == 0:
                    b.emit("scalar", lambda e, dst=dst, psrc=psrc, gcol=gcol, bcol=bcol:
                           e.activation(dst, psrc, AF.Identity, bias=bcol, scale=gcol))
                else:
                    b.emit("vector", lambda e, dst=dst, psrc=psrc, gcol=gcol, bcol=bcol:
                           e.tensor_scalar(dst, psrc, gcol, bcol, ALU.mult, ALU.add))
            etok2 = b.tok("ac") if par == 0 else b.tok("ve")
            pfree[id(px2)] = etok2
            dsta = acc[:, :].rearrange("q (c t) -> q c t", t=TH)[:, :, tt * 128:(tt + 1) * 128]
            xadd = xn[:, :].rearrange("q (c t) -> q c t", t=TH)[:, :, tt * 128:(tt + 1) * 128]
            b.wait("vector", etok1, *acc_ld)
            ev3 = b.emit("vector", lambda e, dsta=dsta, xadd=xadd:
                         e.tensor_tensor(dsta, dsta, xadd, ALU.add), inc="ve")
            x_busy[tt % 4] = ptok1

        front(0)
        front(1)
        acc_ld.extend(issue_acc_ld())
        front(2)
        for tt in range(3, NT128):
            back(tt - 3)
            front(tt)
        back(NT128 - 3)
        back(NT128 - 2)
        back(NT128 - 1)


def _u_stage(b, nc, h, w1_d, w2_d, s1c, s2c, xn, h_sb, wub, guard):
    with ExitStack() as uctx:
        w1b, w2b = wub[0:3], wub[3:6]
        silu_t = [b.sbuf(f"silu{i}_h{h}", [128, 512], F32, uctx) for i in range(2)]
        pu = [b.psum(f"pu{i}_h{h}", [128, 512], F32, uctx) for i in range(8)]
        consumed = {}
        pu_free = {}
        silu_free = {}
        b.wait("gpsimd", guard)
        for oc in range(NHC):
            buf = oc % 3
            b.wait("gpsimd", consumed.get(buf))
            wt1 = b.GQ.issue(lambda e, oc=oc, buf=buf: e.dma_start(
                out=w1b[buf][:, :], in_=w1_d[oc].rearrange("p c o -> p (c o)")))
            wt2 = b.GQ.issue(lambda e, oc=oc, buf=buf: e.dma_start(
                out=w2b[buf][:, :], in_=w2_d[oc].rearrange("p c o -> p (c o)")))
            banks = [pu[(2 * oc) % 8], pu[(2 * oc + 1) % 8],
                     pu[(2 * oc + 4) % 8], pu[(2 * oc + 5) % 8]]
            b.wait("tensor", wt1, wt2, *[pu_free.get(id(bk)) for bk in banks])
            for wsel, wb in ((0, w1b[buf]), (1, w2b[buf])):
                for ic in range(NDC):
                    lhs = wb[:, ic * 128:(ic + 1) * 128]
                    for ti in range(NTT):
                        bank = banks[2 * wsel + ti]
                        rhs = xn[:, ic * TH + ti * 512: ic * TH + (ti + 1) * 512]
                        b.emit("tensor", lambda e, bank=bank, lhs=lhs, rhs=rhs,
                               st=(ic == 0), sp=(ic == NDC - 1):
                               e.matmul(bank[:, :], lhs, rhs, start=st, stop=sp))
            mm_tok = b.tok("pe")
            consumed[buf] = mm_tok
            for ti in range(NTT):
                st = silu_t[ti]
                scol = s1c[:, oc:oc + 1]
                s2col = s2c[:, oc:oc + 1]
                u1bank = banks[ti]
                u2bank = banks[2 + ti]
                b.wait("scalar", mm_tok, silu_free.get(id(st)))
                atok = b.emit("scalar", lambda e, st=st, u1bank=u1bank, scol=scol:
                              e.activation(st[:, :], u1bank[:, :], AF.Silu, scale=scol),
                              inc="ac")
                dst = h_sb[:, oc * TH + ti * 512: oc * TH + (ti + 1) * 512]
                b.wait("vector", atok)
                vtok = b.emit("vector", lambda e, dst=dst, u2bank=u2bank, s2col=s2col, st=st:
                              e.scalar_tensor_tensor(dst, u2bank[:, :], s2col, st[:, :],
                                                     ALU.mult, ALU.mult), inc="ve")
                silu_free[id(st)] = vtok
                pu_free[id(u1bank)] = atok
                pu_free[id(u2bank)] = vtok
        b.barrier()


def _attn_phase(b, nc, h, wq_d, wk_d, wv_d, wa_d, wo_d, ban, cc, e2, e2b, zeros64,
                mask2, ident_b, xn2, acc, S_sb, S_bf, tap):
    with ExitStack() as actx:
        v_tm = b.sbuf(f"v_tm_h{h}", [128, NT128 * DI], BF16, actx)
        o_sb = b.sbuf(f"o_sb_h{h}", [128, NIC * TH], BF16, actx)
        pa = [b.psum(f"pa{i}_h{h}", [128, 512], F32, actx) for i in range(6)]
        ptb = [b.psum(f"ptb{i}_h{h}", [128, 1024], BF16, actx) for i in range(2)]
        pa_free = {}
        ptb_free = {}

        def use_bank(bank, eng="tensor"):
            b.wait(eng, pa_free.get(id(bank)))

        PP = []
        for p in range(NIC):
            def qsb(name, shape, dt, p=p):
                return b.sbuf(f"{name}_q{p}_h{h}", shape, dt, actx)
            PP.append(dict(
                q=qsb("qt", [128, TH], BF16),
                ktm=qsb("ktm", [128, NCH * 128], BF16),
                pm=qsb("pm", [128, NCH * 2 * CHUNK], BF16),
                cend=qsb("cend", [128, NCH], F32),
            ))
        B2 = []
        for par in range(2):
            def psb(name, shape, dt, par=par):
                return b.sbuf(f"{name}_s{par}_h{h}", shape, dt, actx)
            B2.append(dict(
                w0=psb("w0", [128, NDC * 128], BF16),
                w1=psb("w1", [128, NDC * 128], BF16),
                k=psb("kp", [128, TH], BF16),
                tsc=psb("tscp", [128, TH], F32),
                cinv=psb("cinvp", [128, TH], BF16),
                k2=psb("k2p", [128, TH], BF16),
                sqn=psb("sqnp", [2, TH], F32),
                rn=psb("rnp", [2, TH], BF16),
            ))
        a_end = {}
        a_ktm = {}
        v_done = {}

        def stageA(p):
            yield
            par = p % 2
            Bp = B2[par]
            Pp = PP[p]
            Bp["kr"] = Bp["cinv"]
            bankA, bankB, bankC = pa[3 * par], pa[3 * par + 1], pa[3 * par + 2]
            tb = ptb[par]
            cbf = Pp["ktm"]
            if p - 2 in a_end:
                for eng in ("tensor", "vector", "scalar", "gpsimd"):
                    b.wait(eng, *a_end[p - 2])
            wa_t = b.GQ.issue(lambda e, dst=Bp["w0"], p=p: e.dma_start(
                out=dst[:, :], in_=wa_d[p].rearrange("q c o -> q (c o)")))
            wk_t = b.GQ.issue(lambda e, dst=Bp["w1"], p=p: e.dma_start(
                out=dst[:, :], in_=wk_d[p].rearrange("q c o -> q (c o)")))

            def proj(wsb, wtok, dst, bias_col, func):
                toks = []
                for ti in range(NTT):
                    bank = bankA if ti == 0 else bankB
                    use_bank(bank)
                    b.wait("tensor", wtok)
                    for ic in range(NDC):
                        lhs = wsb[:, ic * 128:(ic + 1) * 128]
                        rhs = xn2[:, ic * TH + ti * 512: ic * TH + (ti + 1) * 512]
                        b.emit("tensor", lambda e, bank=bank, lhs=lhs, rhs=rhs,
                               st=(ic == 0), sp=(ic == NDC - 1):
                               e.matmul(bank[:, :], lhs, rhs, start=st, stop=sp))
                    mt = b.tok("pe")
                    dsl = dst[:, ti * 512:(ti + 1) * 512]
                    b.wait("scalar", mt)
                    if func is None:
                        et = b.emit("scalar", lambda e, dsl=dsl, bank=bank:
                                    e.copy(dsl, bank[:, :]), inc="ac")
                    else:
                        et = b.emit("scalar", lambda e, dsl=dsl, bank=bank,
                                    func=func, bias_col=bias_col:
                                    e.activation(dsl, bank[:, :], func,
                                                 bias=bias_col, scale=-1.0), inc="ac")
                    pa_free[id(bank)] = et
                    toks.append(et)
                return toks

            ztoks = proj(Bp["w0"], wa_t, Bp["tsc"], ban[:, p:p + 1], AF.Exp)
            b.wait("scalar", *ztoks)
            sp_tok = b.emit("scalar", lambda e, tsc=Bp["tsc"]:
                            e.activation(tsc[:, :], tsc[:, :], AF.Ln, bias=1.0), inc="ac")
            yield
            ktoks = proj(Bp["w1"], wk_t, Bp["k"], None, None)
            b.wait("gpsimd", b.tok("pe"))
            wv_t = b.GQ.issue(lambda e, dst=Bp["w1"], p=p: e.dma_start(
                out=dst[:, :], in_=wv_d[p].rearrange("q c o -> q (c o)")))
            b.wait("vector", sp_tok)
            for n in range(NCH):
                sl = Bp["tsc"][:, n * CHUNK:(n + 1) * CHUNK]
                b.emit("vector", lambda e, sl=sl: e.tensor_tensor_scan(
                    sl, sl, zeros64[:, :], 0.0, ALU.add, ALU.add))
            scan_tok = b.tok("ve")
            yield
            b.wait("scalar", scan_tok)
            b.emit("scalar", lambda e, cbf=cbf, tsc=Bp["tsc"]: e.activation(
                cbf[:, :], tsc[:, :], AF.Exp, scale=-1.0))
            cend_src = Bp["tsc"][:, :].rearrange("q (n c) -> q n c", c=CHUNK)[:, :, CHUNK - 1]
            ce_tok = b.emit("scalar", lambda e, cend=Pp["cend"], cend_src=cend_src:
                            e.activation(cend[:, :], cend_src, AF.Exp, scale=-1.0),
                            inc="ac")
            b.wait("vector", ce_tok)
            cl_tok = b.emit("vector", lambda e, tsc=Bp["tsc"]: e.tensor_scalar(
                tsc[:, :], tsc[:, :], CLAMP, None, ALU.min), inc="ve")
            b.wait("scalar", cl_tok)
            ci_tok = b.emit("scalar", lambda e, cinv=Bp["cinv"], tsc=Bp["tsc"]:
                            e.activation(cinv[:, :], tsc[:, :], AF.Exp), inc="ac")
            yield
            b.wait("vector", *ktoks)
            k2tok = b.emit("vector", lambda e, k2=Bp["k2"], k_sb=Bp["k"]: e.tensor_tensor(
                k2[:, :], k_sb[:, :], k_sb[:, :], ALU.mult), inc="ve")
            b.wait("tensor", k2tok)
            sq_banks = [bankC, bankA]
            for ti in range(NTT):
                bank = sq_banks[ti]
                use_bank(bank)
                rhs = Bp["k2"][:, ti * 512:(ti + 1) * 512]
                b.emit("tensor", lambda e, bank=bank, rhs=rhs: e.matmul(
                    bank[0:2, 0:512], e2[:, :], rhs, start=True, stop=True))
            mt = b.tok("pe")
            b.wait("vector", mt)
            sq_tok = None
            for ti in range(NTT):
                bank = sq_banks[ti]
                dst = Bp["sqn"][:, ti * 512:(ti + 1) * 512]
                sq_tok = b.emit("vector", lambda e, dst=dst, bank=bank: e.tensor_copy(
                    dst, bank[0:2, 0:512]), inc="ve")
            pa_free[id(bankC)] = sq_tok
            pa_free[id(bankA)] = sq_tok
            b.wait("scalar", sq_tok)
            b.emit("scalar", lambda e, sqn=Bp["sqn"]: e.activation(
                sqn[:, :], sqn[:, :], AF.Ln, bias=cc[0:2, 1:2]))
            rn_tok = b.emit("scalar", lambda e, rn=Bp["rn"], sqn=Bp["sqn"]: e.activation(
                rn[:, :], sqn[:, :], AF.Exp, scale=-0.5), inc="ac")
            yield
            b.wait("tensor", rn_tok)
            kn_tok = None
            for ti in range(NTT):
                bank = sq_banks[ti]
                use_bank(bank)
                rhs = Bp["rn"][:, ti * 512:(ti + 1) * 512]
                b.emit("tensor", lambda e, bank=bank, rhs=rhs: e.matmul(
                    bank[:, :], e2b[:, :], rhs, start=True, stop=True))
                mt = b.tok("pe")
                ksl = Bp["k"][:, ti * 512:(ti + 1) * 512]
                b.wait("vector", mt)
                kn_tok = b.emit("vector", lambda e, ksl=ksl, bank=bank:
                                e.scalar_tensor_tensor(ksl, bank[:, :], 0.0, ksl,
                                                       ALU.bypass, ALU.mult), inc="ve")
                pa_free[id(bank)] = kn_tok
            yield
            wq_t = b.GQ.issue(lambda e, dst=Bp["w0"], p=p: e.dma_start(
                out=dst[:, :], in_=wq_d[p].rearrange("q c o -> q (c o)")))
            qtoks = proj(Bp["w0"], wq_t, Pp["q"], None, None)
            b.wait("vector", *qtoks, ci_tok)
            b.emit("vector", lambda e, q_sb=Pp["q"], cbf=cbf: e.tensor_tensor(
                q_sb[:, :], q_sb[:, :], cbf[:, :], ALU.mult))
            qk_tok = b.emit("vector", lambda e, k_sb=Bp["k"], cinv=Bp["cinv"]:
                            e.tensor_tensor(k_sb[:, :], k_sb[:, :], cinv[:, :], ALU.mult),
                            inc="ve")
            yield
            b.wait("vector", qk_tok)
            krts = []
            for n in range(NCH):
                krsl = Bp["kr"][:, n * CHUNK:(n + 1) * CHUNK]
                ksl = Bp["k"][:, n * CHUNK:(n + 1) * CHUNK]
                ccol = Pp["cend"][:, n:n + 1]
                krts.append(b.emit("vector", lambda e, krsl=krsl, ksl=ksl, ccol=ccol:
                            e.tensor_scalar(krsl, ksl, ccol, None, ALU.mult), inc="ve"))
            yield
            b.wait("tensor", wv_t)
            for t128 in range(NT128):
                bank = bankA if t128 % 2 == 0 else bankB
                use_bank(bank)
                for ic in range(NDC):
                    lhs = xn2[:, ic * TH + t128 * 128: ic * TH + (t128 + 1) * 128]
                    rhs = Bp["w1"][:, ic * 128:(ic + 1) * 128]
                    b.emit("tensor", lambda e, bank=bank, lhs=lhs, rhs=rhs,
                           st=(ic == 0), sp=(ic == NDC - 1):
                           e.matmul(bank[:, 0:128], lhs, rhs, start=st, stop=sp))
                mt = b.tok("pe")
                dst = v_tm[:, t128 * DI + p * 128:t128 * DI + (p + 1) * 128]
                b.wait("scalar", mt)
                et = b.emit("scalar", lambda e, dst=dst, bank=bank: e.copy(
                    dst, bank[:, 0:128]), inc="ac")
                pa_free[id(bank)] = et
                if t128 % 2 == 1:
                    yield
            v_done[p] = b.tok("ac")
            b.wait("tensor", qk_tok)
            for n in range(NCH):
                use_bank(bankA)
                use_bank(bankB)
                lhs0 = Bp["k"][0:64, n * CHUNK:(n + 1) * CHUNK]
                rhs0 = Pp["q"][0:64, n * CHUNK:(n + 1) * CHUNK]
                lhs1 = Bp["k"][64:128, n * CHUNK:(n + 1) * CHUNK]
                rhs1 = Pp["q"][64:128, n * CHUNK:(n + 1) * CHUNK]
                b.emit("tensor", lambda e, bankA=bankA, lhs0=lhs0, rhs0=rhs0:
                       e.matmul(bankA[:, 0:128], lhs0, rhs0, start=True, stop=True,
                                tile_position=(0, 0)))
                b.emit("tensor", lambda e, bankB=bankB, lhs1=lhs1, rhs1=rhs1:
                       e.matmul(bankB[:, 0:128], lhs1, rhs1, start=True, stop=True,
                                tile_position=(64, 0)))
                pp_tok = b.tok("pe")
                b.wait("vector", pp_tok)
                pd0 = Pp["pm"][:, n * 256: n * 256 + 128]
                pd1 = Pp["pm"][:, n * 256 + 128: (n + 1) * 256]
                b.emit("vector", lambda e, pd0=pd0, bankA=bankA:
                       e.tensor_tensor(pd0, bankA[:, 0:128], mask2[:, :], ALU.mult))
                ev2 = b.emit("vector", lambda e, pd1=pd1, bankB=bankB:
                             e.tensor_tensor(pd1, bankB[:, 0:128], mask2[:, :], ALU.mult),
                             inc="ve")
                pa_free[id(bankA)] = ev2
                pa_free[id(bankB)] = ev2
                if n % 2 == 1:
                    yield
            b.wait("tensor", krts[-1], ptb_free.get(id(tb)))
            for n in range(NCH):
                tdst = tb[:, n * 128:(n + 1) * 128]
                krsl = Bp["kr"][:, n * CHUNK:(n + 1) * CHUNK]
                b.emit("tensor", lambda e, tdst=tdst, krsl=krsl:
                       e.transpose(tdst, krsl, ident_b[:, :]))
            ttk = b.tok("pe")
            b.wait("scalar", ttk)
            ev1 = b.emit("scalar", lambda e, kdst=Pp["ktm"], tb=tb:
                         e.copy(kdst[:, :], tb[:, :]), inc="ac")
            ptb_free[id(tb)] = ev1
            a_ktm[p] = ev1
            a_end[p] = (b.tok("pe"), b.tok("ve"), b.tok("ac"))

        def drain(g):
            if g is not None:
                for _ in g:
                    pass

        def interleave(g1, g2):
            while True:
                s1 = next(g1, StopIteration) is StopIteration if g1 else True
                s2 = next(g2, StopIteration) is StopIteration if g2 else True
                if s1 and s2:
                    break

        a_gens = [stageA(p) for p in range(NIC)]
        interleave(a_gens[0], a_gens[1])
        interleave(a_gens[2], a_gens[3])

        sbf_tok = {p: None for p in range(NIC)}
        for n in range(NCH):
            toff = n * DI
            for p in range(NIC):
                Pp = PP[p]
                Sbfp = S_bf[p]
                kv_bank = pa[p % 3]
                o_bank = pa[3 + p % 3]
                use_bank(kv_bank)
                b.wait("tensor", a_ktm.get(p), v_done.get(p))
                klhs = Pp["ktm"][:, n * 128:(n + 1) * 128]
                vrhs = v_tm[:, toff + p * 128: toff + (p + 1) * 128]
                b.emit("tensor", lambda e, kv_bank=kv_bank, klhs=klhs, vrhs=vrhs:
                       e.matmul(kv_bank[:, 0:128], klhs, vrhs, start=True, stop=True))
                kv_tok = b.tok("pe")
                use_bank(o_bank)
                b.wait("tensor", sbf_tok[p])
                qsl = Pp["q"][:, n * CHUNK:(n + 1) * CHUNK]
                v0 = v_tm[:, toff + p * 128: toff + p * 128 + 64]
                v1 = v_tm[:, toff + p * 128 + 64: toff + (p + 1) * 128]
                pm0 = Pp["pm"][:, n * 256: n * 256 + 128]
                pm1 = Pp["pm"][:, n * 256 + 128: (n + 1) * 256]
                b.emit("tensor", lambda e, o_bank=o_bank, v0=v0, pm0=pm0:
                       e.matmul(o_bank[0:64, 0:CHUNK], v0, pm0, start=True, stop=False,
                                tile_position=(0, 0), skip_group_check=True))
                b.emit("tensor", lambda e, o_bank=o_bank, v1=v1, pm1=pm1:
                       e.matmul(o_bank[64:128, 0:CHUNK], v1, pm1, start=True, stop=False,
                                tile_position=(0, 64), skip_group_check=True))
                b.emit("tensor", lambda e, o_bank=o_bank, Sbfp=Sbfp, qsl=qsl:
                       e.matmul(o_bank[:, 0:CHUNK], Sbfp[:, :], qsl,
                                start=False, stop=True, skip_group_check=True))
                o_tok = b.tok("pe")
                ccol = Pp["cend"][:, n:n + 1]
                b.wait("vector", kv_tok, o_tok)
                b.emit("vector", lambda e, Sbfp=Sbfp, kv_bank=kv_bank, ccol=ccol:
                       e.scalar_tensor_tensor(Sbfp[0:64, 0:64], Sbfp[0:64, 0:64],
                                              ccol[0:64, :], kv_bank[0:64, 0:64],
                                              ALU.mult, ALU.add))
                kv_done = b.emit("vector", lambda e, Sbfp=Sbfp, kv_bank=kv_bank, ccol=ccol:
                       e.scalar_tensor_tensor(Sbfp[64:128, 64:128], Sbfp[64:128, 64:128],
                                              ccol[64:128, :], kv_bank[64:128, 64:128],
                                              ALU.mult, ALU.add), inc="ve")
                sbf_tok[p] = kv_done
                pa_free[id(kv_bank)] = kv_done
                odst = o_sb[:, p * TH + n * CHUNK: p * TH + (n + 1) * CHUNK]
                b.wait("scalar", o_tok)
                oev = b.emit("scalar", lambda e, odst=odst, o_bank=o_bank:
                             e.copy(odst, o_bank[:, 0:CHUNK]), inc="ac")
                pa_free[id(o_bank)] = oev

        b.barrier(engs=("tensor", "vector", "scalar"))
        tap(f"o_h{h}", o_sb[:, :], b.tok("ve"), b.tok("ac"))

        wo_bufs = [B2[0]["w0"], B2[0]["w1"], B2[1]["w0"], B2[1]["w1"]]
        for eng in ("gpsimd",):
            b.wait(eng, *a_end.get(2, ()), *a_end.get(3, ()))
        wo_toks = []
        for ic in range(NIC):
            wo_toks.append(b.GQ.issue(lambda e, dst=wo_bufs[ic], ic=ic: e.dma_start(
                out=dst[:, :], in_=wo_d[:, ic, :])))
        b.wait("tensor", *wo_toks)
        for dc in range(NDC):
            for ti in range(NTT):
                bank = pa[(dc * NTT + ti) % 4]
                use_bank(bank)
                for ic in range(NIC):
                    lhs = wo_bufs[ic][:, dc * 128:(dc + 1) * 128]
                    rhs = o_sb[:, ic * TH + ti * 512: ic * TH + (ti + 1) * 512]
                    b.emit("tensor", lambda e, bank=bank, lhs=lhs, rhs=rhs,
                           st=(ic == 0), sp=(ic == NIC - 1):
                           e.matmul(bank[:, :], lhs, rhs, start=st, stop=sp))
                mt = b.tok("pe")
                asl = acc[:, dc * TH + ti * 512: dc * TH + (ti + 1) * 512]
                b.wait("vector", mt)
                ev = b.emit("vector", lambda e, asl=asl, bank=bank:
                            e.scalar_tensor_tensor(asl, bank[:, :], 0.0, asl,
                                                   ALU.bypass, ALU.add), inc="ve")
                pa_free[id(bank)] = ev
        b.barrier(engs=("tensor", "vector", "scalar"))


def _ffn_tail(b, nc, h, h1_d, h2_d, w3_d, hs1c, hs2c, s3c, h_sb, acc, tap,
              t0h, out_d):
    store_toks = [None] * NDC
    with ExitStack() as fctx:
        g1 = b.sbuf(f"g1_h{h}", [128, NHC * TH], BF16, fctx)
        hw = [b.sbuf(f"hw{i}_h{h}", [128, NHC * 128], BF16, fctx) for i in range(3)]
        ph = [b.psum(f"ph{i}_h{h}", [128, 512], F32, fctx) for i in range(8)]
        ph_free = {}
        wconsumed = {}
        b.dma_gate("gpsimd")

        def h_stage(wd, nic_chunks, src, dst, dst_nch, scale_sb, mode):
            nh = nic_chunks // 2
            for oc in range(dst_nch):
                buf = oc % 3
                b.wait("gpsimd", wconsumed.get(buf))
                wt0 = b.GQ.issue(lambda e, oc=oc, buf=buf, wd=wd, nh=nh:
                                 e.dma_start(out=hw[buf][:, 0:nh * 128],
                                             in_=wd[oc][:, 0:nh, :].rearrange("p c o -> p (c o)")))
                wt1 = b.GQ.issue(lambda e, oc=oc, buf=buf, wd=wd, nh=nh, nic_chunks=nic_chunks:
                                 e.dma_start(out=hw[buf][:, nh * 128:nic_chunks * 128],
                                             in_=wd[oc][:, nh:nic_chunks, :].rearrange("p c o -> p (c o)")))
                ets = []
                for ti in range(NTT):
                    bank = ph[(NTT * oc + ti) % 8]
                    b.wait("tensor", ph_free.get(id(bank)))
                    for ic in range(nic_chunks):
                        b.wait("tensor", wt0 if ic == 0 else (wt1 if ic == nh else None))
                        lhs = hw[buf][:, ic * 128:(ic + 1) * 128]
                        rhs = src[:, ic * TH + ti * 512: ic * TH + (ti + 1) * 512]
                        b.emit("tensor", lambda e, bank=bank, lhs=lhs, rhs=rhs,
                               st=(ic == 0), sp=(ic == nic_chunks - 1):
                               e.matmul(bank[:, :], lhs, rhs, start=st, stop=sp))
                    mt = b.tok("pe")
                    wconsumed[buf] = mt
                    scol = scale_sb[:, oc:oc + 1]
                    if mode == "gelu":
                        ddst = dst[:, oc * TH + ti * 512: oc * TH + (ti + 1) * 512]
                        b.wait("scalar", mt)
                        et = b.emit("scalar", lambda e, ddst=ddst, bank=bank, scol=scol:
                                    e.activation(ddst, bank[:, :], AF.Gelu, scale=scol),
                                    inc="ac")
                    else:
                        asl = acc[:, oc * TH + ti * 512: oc * TH + (ti + 1) * 512]
                        b.wait("vector", mt)
                        et = b.emit("vector", lambda e, asl=asl, bank=bank, scol=scol:
                                    e.scalar_tensor_tensor(asl, bank[:, :], scol, asl,
                                                           ALU.mult, ALU.add), inc="ve")
                    ph_free[id(bank)] = et
                    ets.append(et)
                if mode == "w3":
                    srcp = acc[:, oc * TH:(oc + 1) * TH]
                    dstp = out_d[oc * 128:(oc + 1) * 128, t0h:t0h + TH]
                    b.wait("sync", *ets)
                    store_toks[oc] = b.ST.issue(
                        lambda e, dstp=dstp, srcp=srcp: e.dma_start(out=dstp, in_=srcp))

        h_stage(h1_d, NHC, h_sb, g1, NHC, hs1c, "gelu")
        b.barrier(engs=("tensor", "vector", "scalar"))
        tap(f"g1_h{h}", g1[:, :], b.tok("ac"))
        h_stage(h2_d, NHC, g1, h_sb, NHC, hs2c, "gelu")
        b.barrier(engs=("tensor", "vector", "scalar"))
        b.h2_done = [b.tok("pe"), b.tok("ac"), b.tok("ve")]
        h_stage(w3_d, NHC, h_sb, None, NDC, s3c, "w3")
        b.barrier()
    return store_toks


def _prep_weights(inputs):
    w = {}
    w["wq"] = np.ascontiguousarray(np.asarray(inputs["L0_attn_Wq"]).reshape(NDC, 128, NIC, 128).transpose(2, 1, 0, 3))
    w["wk"] = np.ascontiguousarray(np.asarray(inputs["L0_attn_Wk"]).reshape(NDC, 128, NIC, 128).transpose(2, 1, 0, 3))
    w["wv"] = np.ascontiguousarray(np.asarray(inputs["L0_attn_Wv"]).reshape(NDC, 128, NIC, 128).transpose(2, 1, 0, 3))
    w["wa"] = np.ascontiguousarray(np.asarray(inputs["L0_attn_Wa"]).reshape(NDC, 128, NIC, 128).transpose(2, 1, 0, 3))
    w["wo"] = np.ascontiguousarray(np.asarray(inputs["L0_attn_Wo"]).reshape(NIC, 128, D).transpose(1, 0, 2))
    w["ba"] = np.ascontiguousarray(np.asarray(inputs["L0_attn_ba"]))
    w["gamma"] = np.ascontiguousarray(np.asarray(inputs["L0_attn_gamma"]))
    w["beta"] = np.ascontiguousarray(np.asarray(inputs["L0_attn_beta"]))
    w["w1t"] = np.ascontiguousarray(np.asarray(inputs["L0_ffn_w1_w"]).T.reshape(NDC, 128, NHC, 128).transpose(2, 1, 0, 3))
    w["w2t"] = np.ascontiguousarray(np.asarray(inputs["L0_ffn_w2_w"]).T.reshape(NDC, 128, NHC, 128).transpose(2, 1, 0, 3))
    w["h1t"] = np.ascontiguousarray(np.asarray(inputs["L0_ffn_h1_w"]).T.reshape(NHC, 128, NHC, 128).transpose(2, 1, 0, 3))
    w["h2t"] = np.ascontiguousarray(np.asarray(inputs["L0_ffn_h2_w"]).T.reshape(NHC, 128, NHC, 128).transpose(2, 1, 0, 3))
    w["w3t"] = np.ascontiguousarray(np.asarray(inputs["L0_ffn_w3_w"]).T.reshape(NHC, 128, NDC, 128).transpose(2, 1, 0, 3))
    w["s1"] = np.ascontiguousarray(np.asarray(inputs["L0_ffn_w1_s"]))
    w["s2"] = np.ascontiguousarray(np.asarray(inputs["L0_ffn_w2_s"]))
    w["hs1"] = np.ascontiguousarray(np.asarray(inputs["L0_ffn_h1_s"]))
    w["hs2"] = np.ascontiguousarray(np.asarray(inputs["L0_ffn_h2_s"]))
    w["s3"] = np.ascontiguousarray(np.asarray(inputs["L0_ffn_w3_s"]))
    return w


def kernel(**inputs):
    x = np.asarray(inputs["L0_attn_x"])
    w = _prep_weights(inputs)
    nc, _ = build_program()
    in_maps = []
    for c in range(NCORES):
        m = {"x": np.ascontiguousarray(x[c]),
             "xf": np.ascontiguousarray(x[c].T)}
        m.update(w)
        in_maps.append(m)
    res = run_bass_kernel_spmd(nc, in_maps, core_ids=list(range(NCORES)))
    out = np.stack([r["out"].T for r in res.results], axis=0)
    return np.ascontiguousarray(out).astype(np.float32)
